# version 04
# baseline (speedup 1.0000x reference)
import math
import numpy as np
import ml_dtypes
import concourse.bass as bass
import concourse.mybir as mybir
from concourse.bass_utils import run_bass_kernel_spmd
from contextlib import ExitStack

F32 = mybir.dt.float32
BF16 = mybir.dt.bfloat16
AF = mybir.ActivationFunctionType
ALU = mybir.AluOpType

SEQ, D, NH = 4096, 1024, 16
NB = 1366
NR = [1366, 1365, 1365]
OFF = [0, 1366, 2731]
NT = 11
NBP = 1408
SHIFT = 8.0
KC = 8
EPS = 1e-5
NE = 9
ENGS = ("t", "v", "s", "g", "y")


def qchunks(n, w=512):
    out, o = [], 0
    while o < n:
        c = min(w, n - o)
        out.append((o, c))
        o += c
    return out


def sg_ranges(r0, r1):
    bounds = [0, 1366, 2731, 4096]
    for s in range(3):
        a, b = max(r0, bounds[s]), min(r1, bounds[s + 1])
        if a < b:
            yield a, b - a, 3 * (a - OFF[s]) + s


class Sem:
    def __init__(self, nc, st, name):
        self.h = st.enter_context(nc.semaphore(name))
        self.v = 0

    def bump(self, n=1):
        self.v += n
        return (self, self.v)


class Prog:
    def __init__(self):
        self.q = {k: [] for k in ENGS}
        self.wm = {k: {} for k in ENGS}

    def op(self, ek, builder, waits=(), inc=None, inc_n=1):
        wm = self.wm[ek]
        ws = []
        for t in waits:
            if t is None:
                continue
            sm, val = t
            if wm.get(sm, 0) < val:
                wm[sm] = val
                ws.append((sm.h, val))
        tok = None
        inc_args = None
        if inc is not None:
            tok = inc.bump(inc_n)
            inc_args = (inc.h, inc_n)
        def fn(eng, b=builder, ws=tuple(ws), ia=inc_args):
            for (h, v) in ws:
                eng.wait_ge(h, v)
            ins = b(eng)
            if ia is not None:
                ins.then_inc(*ia)
        self.q[ek].append(fn)
        return tok

    def wait(self, ek, *tokens):
        wm = self.wm[ek]
        ws = []
        for t in tokens:
            if t is None:
                continue
            sm, val = t
            if wm.get(sm, 0) < val:
                wm[sm] = val
                ws.append((sm.h, val))
        if ws:
            def fn(eng, ws=tuple(ws)):
                for (h, v) in ws:
                    eng.wait_ge(h, v)
            self.q[ek].append(fn)


def build(debug=False):
    nc = bass.Bass("TRN2", target_bir_lowering=False, debug=False)

    def din(name, shape, dt=F32):
        return nc.dram_tensor(name, shape, dt, kind="ExternalInput").ap()

    x_h = din("x", [SEQ, D])
    wq_h = din("wq", [D, D]); wk_h = din("wk", [D, D]); wv_h = din("wv", [D, D])
    gwt_h = din("gwt", [D, 8])
    w1_h = din("w1", [8, D, 4096]); b1t_h = din("b1t", [4096, 9])
    w2_h = din("w2", [8, 4096, D]); b2_h = din("b2", [8, D])
    sw1_h = din("sw1", [D, 4096])
    sw2_h = din("sw2", [4096, D]); sb2_h = din("sb2", [D])
    idf_h = din("idf", [128, 128]); idb_h = din("idb", [128, 128], BF16)
    strip_h = din("strip", [128, 132], BF16)
    fmask_h = din("fmask", [128, NB], BF16)
    oh_h = din("oh", [8, 1024], BF16)
    out_h = nc.dram_tensor("out", [SEQ, D], F32, kind="ExternalOutput").ap()

    skind = "ExternalOutput" if debug else "Internal"
    xfm_h = nc.dram_tensor("xfm", [D, SEQ], F32, kind=skind).ap()
    x2fm_h = nc.dram_tensor("x2fm", [D, SEQ], F32, kind=skind).ap()
    xn2fm_h = nc.dram_tensor("xn2fm", [D, SEQ], F32, kind=skind).ap()
    aout_h = nc.dram_tensor("aout", [3, D, NBP], F32, kind=skind).ap()
    outsg_h = nc.dram_tensor("outsg", [SEQ, D], F32, kind="Internal").ap()
    facc_h = nc.dram_tensor("facc", [4, D, 1024], F32, kind=skind).ap() if debug else None
    gdbg_h = {k: nc.dram_tensor("gd_" + k, [128, w], F32, kind=skind).ap()
              for (k, w) in (("l", 8), ("t", 8), ("n", 1), ("p", 8), ("r", 1))} if debug else None
    fct_h = nc.dram_tensor("fct", [4, 9, 1024], BF16, kind=skind).ap() if debug else None
    gall_h = {k: nc.dram_tensor("ga_" + k, [32, 128, w], F32, kind=skind).ap()
              for (k, w) in (("l", 8), ("t", 8), ("n", 1), ("e", 1), ("r", 1),
                             ("d", 1))} if debug else None
    xfm_r = xfm_h.rearrange("(c p) t -> p c t", p=128)
    x2fm_r = x2fm_h.rearrange("(c p) t -> p c t", p=128)
    xn2fm_r = xn2fm_h.rearrange("(c p) t -> p c t", p=128)

    st = ExitStack()

    base0 = (nc.sbuf_base + 511) // 512 * 512
    cur = [base0]
    def sb(name, shape, dt, at=None):
        nel = 1
        for s_ in shape[1:]:
            nel *= s_
        nb = nel * (2 if dt == BF16 else 4)
        if at is None:
            at = (cur[0] + 63) // 64 * 64
            cur[0] = at + nb
        else:
            assert at % 512 == 0
        return nc.alloc_sbuf_tensor_at(name, shape, dt, offset=at)

    gwT = sb("gwT", [128, 8, 8], F32)
    b2e_f = sb("b2e_f", [9, 1024], F32)
    b2e_b = sb("b2e_b", [9, 1024], BF16)
    b1s = sb("b1s", [128, 288], F32)
    oh_s = sb("oh_s", [8, 1024], BF16)
    idf_s = sb("idf_s", [128, 128], F32)
    idb_s = sb("idb_s", [128, 128], BF16)
    onescol = sb("onescol", [128, 1], BF16)
    onesrow = sb("onesrow", [1, 128], BF16)
    ones64 = sb("ones64", [1, 64], BF16)
    strip_s = sb("strip_s", [128, 132], BF16)
    fmask_s = sb("fmask_s", [128, NB], BF16)
    cT = sb("cT", [9, 1024], BF16)
    mean_r = sb("mean_r", [1, 512], F32)
    var_r = sb("var_r", [1, 512], F32)
    rstd_r = sb("rstd_r", [1, 512], F32)
    tmq_r = sb("tmq_r", [1, 512], F32)
    mbf_r = sb("mbf_r", [1, 512], BF16)
    rbf_r = sb("rbf_r", [1, 512], BF16)
    rz_b = sb("rz_b", [1, 1536], BF16)
    l_sb = sb("l_sb", [128, 8], F32)
    top8 = sb("top8", [128, 8], F32)
    negl1 = sb("negl1", [128, 1], F32)
    d12v = sb("d12v", [128, 1], F32)
    e2v = sb("e2v", [128, 1], F32)
    r12 = sb("r12", [128, 1], F32)
    selg = sb("selg", [128, 8], F32)
    peg = sb("peg", [128, 8], F32)
    t1g = sb("t1g", [128, 8], F32)
    cg = sb("cg", [128, 8], F32)
    constv = sb("constv", [128, 2], F32)
    fbv = sb("fbv", [1, 2], F32)
    fbs = sb("fbs", [1, 2], F32)

    AB = (cur[0] + 511) // 512 * 512

    wq_b = sb("wq_b", [128, 8, 1024], BF16, at=AB)
    wk_b = sb("wk_b", [128, 8, 1024], BF16, at=AB + 16384)
    wv_b = sb("wv_b", [128, 8, 1024], BF16, at=AB + 32768)
    q_fm = sb("q_fm", [128, 8, NBP], BF16, at=AB + 49152)
    k_fm = sb("k_fm", [128, 8, NBP], BF16, at=AB + 71680)
    v_ex = sb("v_ex", [128, NT, 1040], BF16, at=AB + 94208)
    xn1 = sb("xn1", [128, 8, NBP], BF16, at=AB + 117248)
    xch = sb("xch", [128, 8, 512], F32, at=AB + 139776)
    xbch = sb("xbch", [128, 8, 512], BF16, at=AB + 156160)
    xsq = sb("xsq", [128, 8, 512], BF16, at=AB + 164352)
    ptb = sb("ptb", [128, 2, NBP], BF16, at=AB + 156160)
    aost = sb("aost", [128, NBP], F32, at=AB + 161792)
    tmpn = sb("tmpn", [128, 512], F32, at=AB + 172544)
    xin = sb("xin", [128, 2, 1024], F32, at=AB + 139776)
    a0st = sb("a0st", [128, 2, 1024], F32, at=AB + 148480)
    ach = sb("ach", [128, 8, 512], F32, at=AB + 49152)
    x2ch = sb("x2ch", [128, 8, 512], F32, at=AB + 65536)
    xn2ch = sb("xn2ch", [128, 8, 512], F32, at=AB + 81920)
    x2bf = sb("x2bf", [128, 8, 512], BF16, at=AB + 98304)
    abf = sb("abf", [128, 8, 512], BF16, at=AB + 106496)
    asq = sb("asq", [128, 8, 512], BF16, at=AB + 114688)
    xn2s = sb("xn2s", [128, 8, 1024], BF16, at=AB)
    accs = sb("accs", [128, 8, 1024], F32, at=AB + 16384)
    h_sb = sb("h_sb", [128, 32, 1024], BF16, at=AB + 49152)
    w1c = sb("w1c", [128, 2, 8, 512], BF16, at=AB + 114688)
    w2c = sb("w2c", [128, 2, 4, 512], BF16, at=AB + 131072)
    cBs = sb("cBs", [128, 1024], F32, at=AB + 139264)
    tev = sb("tev", [128, 2, 512], F32, at=AB + 143360)
    g32 = sb("g32", [128, 2, 8, 128], F32, at=AB + 147456)
    f3x2 = sb("f3x2", [128, 8, 512], F32, at=AB + 49152)
    f3of = sb("f3of", [128, 8, 512], F32, at=AB + 65536)
    f3ab = sb("f3ab", [128, 8, 512], BF16, at=AB + 81920)
    f3sq = sb("f3sq", [128, 8, 512], BF16, at=AB + 90112)
    f3tm = sb("f3tm", [128, 2, 1024], F32, at=AB + 98304)

    assert AB + 174592 <= nc.sbuf_top, f"sbuf overflow AB={AB} top={nc.sbuf_top}"

    pA = st.enter_context(nc.psum_tensor("pA", [128, 512], F32))
    pB = st.enter_context(nc.psum_tensor("pB", [128, 512], F32))
    pC = st.enter_context(nc.psum_tensor("pC", [128, 512], F32))
    pD = st.enter_context(nc.psum_tensor("pD", [128, 512], F32))
    pE = st.enter_context(nc.psum_tensor("pE", [128, 512], F32))
    pF = st.enter_context(nc.psum_tensor("pF", [128, 512], F32))
    pW = st.enter_context(nc.psum_tensor("pW", [128, 1024], F32))

    P = Prog()
    sg = Sem(nc, st, "sg")
    sy = Sem(nc, st, "sy")
    sp = Sem(nc, st, "sp")
    sv = Sem(nc, st, "sv")
    ss = Sem(nc, st, "ss")

    def fence(ek):
        if ek == "v":
            return P.op("v", lambda e: e.tensor_copy(fbv[0:1, 0:1],
                                                     fbv[0:1, 1:2]), inc=sv)
        return P.op("s", lambda e: e.copy(fbs[0:1, 0:1], fbs[0:1, 1:2]),
                    inc=ss)

    BG = {}
    def bank_w_waits(key):
        e = BG.get(key)
        return list(e["rs"]) if e else []
    def bank_note_w(key, tok):
        BG[key] = {"w": tok, "rs": []}
    def bank_r_wait(key):
        e = BG.get(key)
        return e["w"] if e else None
    def bank_note_r(key, tok):
        BG[key]["rs"].append(tok)

    def dma(qk, dst, src, waits=()):
        semq = sg if qk == "g" else sy
        return P.op(qk, lambda e, d=dst, s_=src: e.dma_start(d, s_),
                    waits=waits, inc=semq, inc_n=16)

    def mm(lhsT, rhs, out, key, start, stop, waits=(), force_inc=False):
        w = list(waits)
        if start:
            w += bank_w_waits(key)
        do_inc = stop or force_inc
        tok = P.op("t", lambda e, a=lhsT, b=rhs, o=out, s1=start, s2=stop:
                   e.matmul(o, a, b, start=s1, stop=s2),
                   waits=w, inc=(sp if do_inc else None))
        if stop:
            bank_note_w(key, tok)
        return tok

    def trans(out, in_, ident, key, waits=()):
        w = list(waits) + bank_w_waits(key)
        tok = P.op("t", lambda e, o=out, i=in_, d=ident:
                   e.transpose(o, i, d), waits=w, inc=sp)
        bank_note_w(key, tok)
        return tok

    def pread(ek, builder, key, waits=(), inc=None):
        w = list(waits)
        bw = bank_r_wait(key)
        if bw is not None:
            w.append(bw)
        tok = P.op(ek, builder, waits=w, inc=(inc or (sv if ek == "v" else ss)))
        bank_note_r(key, tok)
        return tok

    tok_idf = dma("y", idf_s[:], idf_h)
    tok_idb = dma("y", idb_s[:], idb_h)
    tok_strip = dma("y", strip_s[:], strip_h)
    tok_fmask = dma("y", fmask_s[:], fmask_h)
    tok_oh = dma("y", oh_s[:], oh_h)
    tok_b2 = dma("y", b2e_f[0:8, :], b2_h)
    tok_sb2 = dma("y", b2e_f[8:9, :], sb2_h.rearrange("(a d) -> a d", a=1))
    tok_gwT = dma("y", gwT[:], gwt_h.rearrange("(c p) e -> p c e", p=128))
    tok_b1s = dma("y", b1s[:].rearrange("p (f e) -> p f e", e=9),
                  b1t_h.rearrange("(f p) e -> p f e", p=128))
    tok_wq = dma("g", wq_b[:], wq_h.rearrange("(c p) o -> p c o", p=128))
    tok_wk = dma("g", wk_b[:], wk_h.rearrange("(c p) o -> p c o", p=128))
    tok_wv = dma("g", wv_b[:], wv_h.rearrange("(c p) o -> p c o", p=128))

    P.op("s", lambda e: e.copy(b2e_b[:], b2e_f[:]),
         waits=[tok_b2, tok_sb2], inc=ss)
    P.op("v", lambda e: e.memset(constv[:, 0:1], EPS))
    P.op("v", lambda e: e.memset(constv[:, 1:2], -SHIFT))
    P.op("v", lambda e: e.memset(cT[:], 1.0))
    P.op("v", lambda e: e.memset(onescol[:], 1.0))
    P.op("v", lambda e: e.memset(onesrow[:], 1.0))
    P.op("v", lambda e: e.memset(ones64[:], 1.0), inc=sv)
    tok_ones = fence("v")
    tok_b2c = fence("s")

    tr_keys = ["W0", "W1"]
    tr_aps = [pW[:, 0:512], pW[:, 512:1024]]
    tok_ld = [None, None]
    tok_st_dma = [None, None]
    for ci in range(32):
        r0 = ci * 128
        bi = ci % 2
        w0 = [tok_st_dma[bi]]
        for (a, n, t0) in sg_ranges(r0, r0 + 128):
            tok_ld[bi] = dma("y", xin[a - r0:a - r0 + n, bi, :],
                             x_h[t0:t0 + 3 * (n - 1) + 1:3, :], waits=w0)
            w0 = []
        cp_toks = []
        for half in range(2):
            for j in range(4):
                dt = half * 4 + j
                trans(tr_aps[half][:, j * 128:(j + 1) * 128],
                      xin[:, bi, dt * 128:(dt + 1) * 128], idf_s[:],
                      tr_keys[half], waits=[tok_ld[bi], tok_idf])
            cp = pread("v", lambda e, bi=bi, half=half: e.tensor_copy(
                a0st[:, bi, half * 512:(half + 1) * 512], tr_aps[half]),
                tr_keys[half], waits=[tok_st_dma[bi]])
            cp_toks.append(cp)
        fa0 = fence("v")
        tok_st_dma[bi] = dma("y", xfm_r[:, :, r0:r0 + 128],
                             a0st[:, bi, :].rearrange("p (c t) -> p c t", c=8),
                             waits=[fa0])
    tok_xfm = (sy, sy.v)

    def ln_feature(xf32, qw, tok_x, bfsrc, tok_bf, sqbuf, cast_eng,
                   bfbuf=None):
        tsq, tcast = [], []
        for kc in range(KC):
            tsq.append(P.op("s", lambda e, kc=kc, qw=qw: e.activation(
                sqbuf[:, kc, 0:qw], xf32[:, kc, 0:qw], AF.Square),
                waits=[tok_x], inc=ss))
        if bfsrc is None:
            bfsrc = bfbuf
            for kc in range(KC):
                tcast.append(P.op(cast_eng, lambda e, kc=kc, qw=qw: (
                    e.copy(bfbuf[:, kc, 0:qw], xf32[:, kc, 0:qw])
                    if cast_eng == "s" else
                    e.tensor_copy(bfbuf[:, kc, 0:qw], xf32[:, kc, 0:qw])),
                    waits=[tok_x], inc=(ss if cast_eng == "s" else sv)))
        fs_pre = fence("s")
        tok_sum = tok_sumsq = None
        for kc in range(KC):
            w = [fs_pre if (tcast and kc == KC - 1) else
                 (tcast[min(kc + 1, KC - 1)] if tcast else tok_bf)]
            tok_sum = mm(onescol[:], bfsrc[:, kc, 0:qw], pA[0:1, 0:qw], "A",
                         kc == 0, kc == KC - 1, waits=w)
        for kc in range(KC):
            w = [fs_pre if kc == KC - 1 else tsq[kc + 1]]
            tok_sumsq = mm(onescol[:], sqbuf[:, kc, 0:qw], pB[0:1, 0:qw], "B",
                           kc == 0, kc == KC - 1, waits=w)
        tmean = pread("v", lambda e, qw=qw: e.tensor_scalar(
            mean_r[0:1, 0:qw], pA[0:1, 0:qw], 1.0 / D, None, ALU.mult),
            "A", waits=[tok_sum])
        tsq_ = P.op("v", lambda e, qw=qw: e.tensor_tensor(
            tmq_r[0:1, 0:qw], mean_r[0:1, 0:qw], mean_r[0:1, 0:qw], ALU.mult),
            waits=[tmean], inc=sv)
        tvar = pread("v", lambda e, qw=qw: e.scalar_tensor_tensor(
            var_r[0:1, 0:qw], pB[0:1, 0:qw], 1.0 / D, tmq_r[0:1, 0:qw],
            ALU.mult, ALU.subtract), "B", waits=[tok_sumsq, tsq_])
        fvv = fence("v")
        tln = P.op("s", lambda e, qw=qw: e.activation(
            tmq_r[0:1, 0:qw], var_r[0:1, 0:qw], AF.Ln,
            bias=constv[0:1, 0:1]), waits=[fvv, tok_ones], inc=ss)
        trs = P.op("s", lambda e, qw=qw: e.activation(
            rstd_r[0:1, 0:qw], tmq_r[0:1, 0:qw], AF.Exp, scale=-0.5), inc=ss)
        fss = fence("s")
        tmb = P.op("v", lambda e, qw=qw: e.tensor_copy(
            mbf_r[0:1, 0:qw], mean_r[0:1, 0:qw]), waits=[tmean], inc=sv)
        trb = P.op("v", lambda e, qw=qw: e.tensor_copy(
            rbf_r[0:1, 0:qw], rstd_r[0:1, 0:qw]), waits=[fss], inc=sv)
        fvb = fence("v")
        tb1 = mm(onesrow[0:1, 0:128], mbf_r[0:1, 0:qw], pC[:, 0:qw], "C",
                 True, True, waits=[fvb, tok_ones])
        tb2 = mm(onesrow[0:1, 0:128], rbf_r[0:1, 0:qw], pD[:, 0:qw], "D",
                 True, True, waits=[fvb])
        return tb1, tb2

    tok_xn1_rd = None
    tok_vex_rd = None
    tok_stage = [None, None]
    tok_pv = {}
    tok_bc = {}
    tok_a3_last = None

    for s in range(3):
        ncol = NR[s]
        qch_real = qchunks(ncol)
        qch_full = qchunks(NB)

        tok_prev_sy = (sy, sy.v) if sy.v else None
        wpad = [tok_xn1_rd, tok_prev_sy]
        tok_pad_xn1 = P.op("v", lambda e, n_=ncol: e.memset(
            xn1[:, :, n_:NBP], 0.0), waits=wpad, inc=sv)
        tok_pad_q = P.op("v", lambda e, n_=ncol: e.memset(
            q_fm[:, :, n_:NBP], 0.0), waits=wpad, inc=sv)
        tok_pad_k = P.op("v", lambda e, n_=ncol: e.memset(
            k_fm[:, :, n_:NBP], 0.0), waits=wpad, inc=sv)
        tok_vex_ones = P.op("v", lambda e: e.memset(v_ex[:], 1.0),
                            waits=[tok_vex_rd, tok_prev_sy], inc=sv)

        tok_norm_last = tok_a3_last if s > 0 else None
        for (qo, qw) in qch_real:
            col0 = OFF[s] + qo
            w0 = [tok_xfm, tok_xn1_rd, tok_stage[0], tok_stage[1],
                  tok_prev_sy] + ([tok_norm_last] if tok_norm_last else [])
            tok_xc = dma("y", xch[:, :, 0:qw], xfm_r[:, :, col0:col0 + qw],
                         waits=w0)
            tok_xb = dma("g", xbch[:, :, 0:qw], xfm_r[:, :, col0:col0 + qw],
                         waits=w0)
            tb1, tb2 = ln_feature(xch, qw, tok_xc, xbch, tok_xb, xsq, None)
            for kc in range(KC):
                pread("v", lambda e, kc=kc, qw=qw: e.tensor_tensor(
                    tmpn[:, 0:qw], xch[:, kc, 0:qw], pC[:, 0:qw],
                    ALU.subtract), "C", waits=[tb1, tb2, tok_pad_xn1])
                tok_norm_last = pread("v", lambda e, kc=kc, qw=qw, qo=qo:
                                      e.tensor_tensor(
                                      xn1[:, kc, qo:qo + qw], tmpn[:, 0:qw],
                                      pD[:, 0:qw], ALU.mult), "D")
        tok_xn1_w = fence("v")

        pq_keys = ["E", "F"]
        pq_aps = [pE, pF]
        qi = 0
        tok_qk_last = None
        for (dst, wsrc, tok_w, tpad) in ((q_fm, wq_b, tok_wq, tok_pad_q),
                                         (k_fm, wk_b, tok_wk, tok_pad_k)):
            for dt in range(8):
                for (qo, qw) in qch_full:
                    key = pq_keys[qi % 2]
                    bank = pq_aps[qi % 2]
                    tstop = None
                    for kc in range(KC):
                        tstop = mm(wsrc[:, kc, dt * 128:(dt + 1) * 128],
                                   xn1[:, kc, qo:qo + qw], bank[:, 0:qw],
                                   key, kc == 0, kc == KC - 1,
                                   waits=[tok_xn1_w, tok_w])
                    tok_qk_last = pread("v", lambda e, dst=dst, dt=dt, qo=qo,
                                        qw=qw, bank=bank: e.tensor_copy(
                                        dst[:, dt, qo:qo + qw], bank[:, 0:qw]),
                                        key, waits=[tstop, tpad])
                    qi += 1
        tok_qk_ev = fence("v")
        tok_v_ev = None
        for tc in range(NT):
            tstop = None
            for kc in range(KC):
                for vh in range(2):
                    key = tr_keys[vh]
                    tstop = mm(xn1[:, kc, tc * 128:(tc + 1) * 128],
                               wv_b[:, kc, vh * 512:(vh + 1) * 512],
                               pW[:, vh * 512:(vh + 1) * 512], key,
                               kc == 0, kc == KC - 1,
                               waits=[tok_xn1_w, tok_wv])
            for vh in range(2):
                tok_v_ev = pread(
                    "v", lambda e, tc=tc, vh=vh: e.tensor_copy(
                        v_ex[:, tc, vh * 520:vh * 520 + 520].rearrange(
                            "p (h c) -> p h c", h=8)[:, :, 0:64],
                        pW[:, vh * 512:(vh + 1) * 512].rearrange(
                            "p (h c) -> p h c", h=8)),
                    tr_keys[vh], waits=[tstop, tok_vex_ones])
        tok_v_ev = fence("v")

        st_keys = ["A", "B"]
        st_aps = [pA, pB]
        o_keys = ["C", "D", "E"]
        o_aps = [pC, pD, pE]
        rz_keys = ["F", "W0", "W1"]
        rz_aps = [pF[0:64, :], pW[0:64, 0:512], pW[0:64, 512:1024]]
        for h in range(NH):
            dtile, p0 = h // 2, (h % 2) * 64
            tok_rdy = {}
            for kc in range(NT):
                par = kc % 2
                for (ci_, (qo, qw)) in enumerate(qch_full):
                    i = kc * 3 + ci_
                    key = st_keys[i % 2]
                    bank = st_aps[i % 2]
                    tst = mm(k_fm[p0:p0 + 64, dtile, kc * 128:(kc + 1) * 128],
                             q_fm[p0:p0 + 64, dtile, qo:qo + qw],
                             bank[0:128, 0:qw], key, True, True,
                             waits=[tok_qk_ev])
                    wpt = [tok_pv.get((par, ci_))]
                    texp = pread("s", lambda e, par=par, qo=qo, qw=qw,
                                 bank=bank: e.activation(
                                 ptb[:, par, qo:qo + qw], bank[0:128, 0:qw],
                                 AF.Exp, bias=constv[:, 1:2], scale=0.125),
                                 key, waits=wpt + [tok_ones])
                    trdy = texp
                    if kc < 10:
                        kb = kc * 128
                        a_, b_ = max(qo, kb - 2), min(qo + qw, kb + 130)
                        if a_ < b_:
                            trdy = P.op("v", lambda e, par=par, a_=a_,
                                        b_=b_, kb=kb: e.tensor_tensor(
                                        ptb[:, par, a_:b_],
                                        ptb[:, par, a_:b_],
                                        strip_s[:, a_ - kb + 2:b_ - kb + 2],
                                        ALU.mult),
                                        waits=[texp, tok_strip], inc=sv)
                    else:
                        trdy = P.op("v", lambda e, par=par, qo=qo, qw=qw:
                                    e.tensor_tensor(
                                    ptb[:, par, qo:qo + qw],
                                    ptb[:, par, qo:qo + qw],
                                    fmask_s[:, qo:qo + qw], ALU.mult),
                                    waits=[texp, tok_fmask], inc=sv)
                    tok_rdy[(kc, ci_)] = trdy
                fkv = fence("v")
                fks = fence("s")
                for (ci_, (qo, qw)) in enumerate(qch_full):
                    tok_pv[(par, ci_)] = mm(
                        v_ex[:, kc, h * 65:(h + 1) * 65],
                        ptb[:, par, qo:qo + qw],
                        o_aps[ci_][0:65, 0:qw], o_keys[ci_],
                        kc == 0, kc == NT - 1,
                        waits=[fkv, fks, tok_v_ev],
                        force_inc=True)
            ev_toks = []
            for (ci_, (qo, qw)) in enumerate(qch_full):
                trz = pread("v", lambda e, ci_=ci_, qw=qw: e.reciprocal(
                    rz_b[0:1, ci_ * 512:ci_ * 512 + qw],
                    o_aps[ci_][64:65, 0:qw]), o_keys[ci_],
                    waits=[tok_bc.get(ci_)])
                frz = fence("v")
                tbc = mm(ones64[0:1, 0:64],
                         rz_b[0:1, ci_ * 512:ci_ * 512 + qw],
                         rz_aps[ci_][:, 0:qw], rz_keys[ci_], True, True,
                         waits=[frz, tok_ones])
                tok_bc[ci_] = tbc
                trc = pread("v", lambda e, ci_=ci_, qw=qw: e.tensor_copy(
                    tmpn[0:64, 0:qw], rz_aps[ci_][:, 0:qw]), rz_keys[ci_],
                    waits=[tbc])
                tev_ = pread("v", lambda e, ci_=ci_, qo=qo, qw=qw, p0=p0:
                             e.tensor_tensor(aost[p0:p0 + 64, qo:qo + qw],
                                             o_aps[ci_][0:64, 0:qw],
                                             tmpn[0:64, 0:qw], ALU.mult),
                             o_keys[ci_], waits=[tok_stage[h % 2], trc])
                ev_toks.append(tev_)
            fev = fence("v")
            tok_stage[h % 2] = dma("y", aout_h[s, h * 64:(h + 1) * 64, 0:NB],
                                   aost[p0:p0 + 64, 0:NB], waits=[fev])
        tok_xn1_rd = (sp, sp.v)
        tok_vex_rd = (sp, sp.v)
        tok_aout_s = [tok_stage[0], tok_stage[1]]

        tok_a3_last = None
        for (qo, qw) in qch_real:
            col0 = OFF[s] + qo
            w0 = [t for t in tok_aout_s if t] + \
                 ([tok_a3_last] if tok_a3_last else [])
            tok_ac = dma("y", ach[:, :, 0:qw],
                         aout_h[s].rearrange("(c p) t -> p c t", p=128)[
                             :, :, qo:qo + qw], waits=w0)
            tok_xc = dma("y", xch[:, :, 0:qw], xfm_r[:, :, col0:col0 + qw],
                         waits=w0)
            tb1, tb2 = ln_feature(ach, qw, tok_ac, None, None, asq, "s",
                                  bfbuf=abf)
            tx2 = None
            for kc in range(KC):
                pread("v", lambda e, kc=kc, qw=qw: e.tensor_tensor(
                    tmpn[:, 0:qw], ach[:, kc, 0:qw], pC[:, 0:qw],
                    ALU.subtract), "C", waits=[tb1, tb2])
                pread("v", lambda e, qw=qw: e.tensor_tensor(
                    tmpn[:, 0:qw], tmpn[:, 0:qw], pD[:, 0:qw], ALU.mult), "D")
                tx2 = P.op("v", lambda e, kc=kc, qw=qw: e.tensor_tensor(
                    x2ch[:, kc, 0:qw], tmpn[:, 0:qw], xch[:, kc, 0:qw],
                    ALU.add), waits=[tok_xc], inc=sv)
            tx2 = fence("v")
            dma("y", x2fm_r[:, :, col0:col0 + qw], x2ch[:, :, 0:qw],
                waits=[tx2])
            tb1, tb2 = ln_feature(x2ch, qw, tx2, None, None, asq, "s",
                                  bfbuf=x2bf)
            txn = None
            for kc in range(KC):
                pread("v", lambda e, kc=kc, qw=qw: e.tensor_tensor(
                    tmpn[:, 0:qw], x2ch[:, kc, 0:qw], pC[:, 0:qw],
                    ALU.subtract), "C", waits=[tb1, tb2])
                txn = pread("v", lambda e, kc=kc, qw=qw: e.tensor_tensor(
                    xn2ch[:, kc, 0:qw], tmpn[:, 0:qw], pD[:, 0:qw],
                    ALU.mult), "D")
            txn = fence("v")
            dma("y", xn2fm_r[:, :, col0:col0 + qw], xn2ch[:, :, 0:qw],
                waits=[txn])
            tok_a3_last = txn
    tok_scr_done = (sy, sy.v)
    tok_attn_pe = (sp, sp.v)

    py_keys = ["A", "B", "C", "D", "E", "F", "W0", "W1"]
    py_aps = [pA, pB, pC, pD, pE, pF, pW[:, 0:512], pW[:, 512:1024]]
    tok_xn2s_rd = None
    tok_gate_act = None
    tok_f3_act = None
    tok_tm_free = [None, None]
    out_dma_toks = []
    tok_w1_rd = [[tok_attn_pe, tok_scr_done], [tok_attn_pe, tok_scr_done]]
    tok_w2_rd = [[tok_attn_pe, tok_scr_done], [tok_attn_pe, tok_scr_done]]
    for tau in range(4):
        t0 = tau * 1024
        w0 = [tok_scr_done] + ([tok_xn2s_rd] if tok_xn2s_rd else [])
        tok_xn2 = dma("g", xn2s[:], xn2fm_r[:, :, t0:t0 + 1024], waits=w0)

        tok_ct = []
        tok_g32_rd = [None, None]
        tok_ttr_prev = None
        for tc in range(8):
            gb = tc % 2
            tok_g32 = dma("y", g32[:, gb],
                          xn2fm_r[:, :, t0 + tc * 128:t0 + (tc + 1) * 128],
                          waits=[tok_scr_done, tok_g32_rd[gb]])
            tstop = None
            for kc in range(KC):
                tstop = mm(g32[:, gb, kc, :],
                           gwT[:, kc, :], pA[0:128, 0:8], "A",
                           kc == 0, kc == KC - 1,
                           waits=[tok_g32, tok_gwT])
            tok_g32_rd[gb] = tstop
            tcp = pread("v", lambda e: e.tensor_copy(l_sb[:], pA[0:128, 0:8]),
                        "A", waits=[tstop, tok_gate_act])
            tmax = P.op("v", lambda e: e.max(top8[:], l_sb[:]),
                        waits=[tcp], inc=sv)
            td12 = P.op("v", lambda e: e.tensor_tensor(
                d12v[:], top8[:, 1:2], top8[:, 0:1], ALU.subtract),
                waits=[tmax], inc=sv)
            tneg = P.op("v", lambda e: e.tensor_scalar(
                negl1[:], top8[:, 0:1], -1.0, None, ALU.mult),
                waits=[tok_gate_act, tmax], inc=sv)
            fg1 = fence("v")
            te2 = P.op("s", lambda e: e.activation(e2v[:], d12v[:], AF.Exp),
                       waits=[fg1], inc=ss)
            tpe = P.op("s", lambda e: e.activation(
                peg[:], l_sb[:], AF.Exp, bias=negl1[:]), inc=ss)
            fg2 = fence("s")
            tden = P.op("v", lambda e: e.tensor_scalar(
                d12v[:], e2v[:], 1.0, None, ALU.add), waits=[fg2], inc=sv)
            tr12 = P.op("v", lambda e: e.reciprocal(r12[:], d12v[:]),
                        waits=[tden], inc=sv)
            tsel = P.op("v", lambda e: e.tensor_scalar(
                selg[:], l_sb[:], top8[:, 1:2], None, ALU.is_ge),
                waits=[tmax, tcp], inc=sv)
            tt1 = P.op("v", lambda e: e.tensor_tensor(
                t1g[:], peg[:], selg[:], ALU.mult), waits=[fg2, tsel], inc=sv)
            tc_ = P.op("v", lambda e: e.tensor_scalar(
                cg[:], t1g[:], r12[:], None, ALU.mult),
                waits=[tr12, tt1, tok_ttr_prev], inc=sv)
            fg3 = fence("v")
            ttr = mm(cg[:], idf_s[:], pB[0:8, 0:128], "B", True, True,
                     waits=[fg3, tok_idf])
            tok_ttr_prev = ttr
            tok_ct.append(pread("v", lambda e, tc=tc: e.tensor_copy(
                cT[0:8, tc * 128:(tc + 1) * 128], pB[0:8, 0:128]), "B",
                waits=[ttr]))
            if debug:
                ci = tau * 8 + tc
                dma("y", gall_h["l"][ci], l_sb[:], waits=[fg3])
                dma("y", gall_h["t"][ci], top8[:], waits=[fg3])
                dma("y", gall_h["n"][ci], negl1[:], waits=[fg3])
                dma("y", gall_h["e"][ci], e2v[:], waits=[fg3])
                dma("y", gall_h["r"][ci], r12[:], waits=[fg3])
                dma("y", gall_h["d"][ci], d12v[:], waits=[fg3])
            tok_gate_act = tpe
        tok_cT = fence("v")

        tok_acc_init = None
        for dt in range(8):
            for tsx in range(2):
                tstop = mm(b2e_b[0:9, dt * 128:(dt + 1) * 128],
                           cT[0:9, tsx * 512:(tsx + 1) * 512],
                           pC[:, 0:512], "C", True, True,
                           waits=[tok_cT, tok_b2c])
                tok_acc_init = pread("v", lambda e, dt=dt, tsx=tsx:
                                     e.tensor_copy(
                                     accs[:, dt, tsx * 512:(tsx + 1) * 512],
                                     pC[:, 0:512]), "C",
                                     waits=[tstop, tok_f3_act])

        P.wait("s", (sy, sy.v))
        tok_acc_ev = tok_acc_init
        for e_ in range(NE):
            shared = (e_ == 8)
            w1src = (sw1_h if shared else w1_h[e_])
            w2src = (sw2_h if shared else w2_h[e_])
            tok_cb = None
            if not shared:
                for tsx in range(2):
                    tstop = mm(oh_s[0:8, e_ * 128:(e_ + 1) * 128],
                               cT[0:8, tsx * 512:(tsx + 1) * 512],
                               pD[:, 0:512], "D", True, True,
                               waits=[tok_cT, tok_oh])
                    tok_cb = pread("v", lambda e, tsx=tsx: e.tensor_copy(
                        cBs[:, tsx * 512:(tsx + 1) * 512], pD[:, 0:512]),
                        "D", waits=[tstop])
            tok_gelu = None
            for fc in range(8):
                buf = fc % 2
                tok_w1 = dma("g", w1c[:, buf],
                             w1src[:, fc * 512:(fc + 1) * 512].rearrange(
                                 "(c p) f -> p c f", p=128),
                             waits=tok_w1_rd[buf])
                for fm in range(4):
                    ft = fc * 4 + fm
                    for ts_ in range(2):
                        key = py_keys[ts_]
                        tstop = None
                        for kc in range(KC):
                            tstop = mm(w1c[:, buf, kc, fm * 128:(fm + 1) * 128],
                                       xn2s[:, kc, ts_ * 512:(ts_ + 1) * 512],
                                       py_aps[ts_][0:128, 0:512], key,
                                       kc == 0, kc == KC - 1,
                                       waits=[tok_w1, tok_xn2])
                        if fm == 3 and ts_ == 1:
                            tok_w1_rd[buf] = [tstop]
                        bcol = ft * 9 + (8 if shared else e_)
                        tok_gelu = pread(
                            "s", lambda e, ft=ft, ts_=ts_, bcol=bcol:
                            e.activation(
                                h_sb[:, ft, ts_ * 512:(ts_ + 1) * 512],
                                py_aps[ts_][0:128, 0:512], AF.Gelu,
                                bias=b1s[:, bcol:bcol + 1]), key,
                            waits=[tok_b1s])
            tok_gelu = fence("s")
            for dh in range(2):
                for fc2 in range(8):
                    buf = fc2 % 2
                    tok_w2 = dma("g", w2c[:, buf],
                                 w2src[fc2 * 512:(fc2 + 1) * 512,
                                       dh * 512:(dh + 1) * 512].rearrange(
                                     "(f p) d -> p f d", p=128),
                                 waits=tok_w2_rd[buf])
                    for fs in range(4):
                        ft = fc2 * 4 + fs
                        for dt2 in range(4):
                            for ts_ in range(2):
                                j = dt2 * 2 + ts_
                                last_of_chunk = (fs == 3 and dt2 == 3
                                                 and ts_ == 1)
                                tstop = mm(
                                    w2c[:, buf, fs, dt2 * 128:(dt2 + 1) * 128],
                                    h_sb[:, ft, ts_ * 512:(ts_ + 1) * 512],
                                    py_aps[j][0:128, 0:512], py_keys[j],
                                    fc2 == 0 and fs == 0,
                                    fc2 == 7 and fs == 3,
                                    waits=[tok_w2, tok_gelu],
                                    force_inc=last_of_chunk)
                                if last_of_chunk:
                                    tok_w2_rd[buf] = [tstop]
                                if fc2 == 7 and fs == 3:
                                    dt = dh * 4 + dt2
                                    if shared:
                                        tok_acc_ev = pread(
                                            "v", lambda e, dt=dt, ts_=ts_, j=j:
                                            e.tensor_tensor(
                                                accs[:, dt, ts_ * 512:
                                                     (ts_ + 1) * 512],
                                                accs[:, dt, ts_ * 512:
                                                     (ts_ + 1) * 512],
                                                py_aps[j][0:128, 0:512],
                                                ALU.add), py_keys[j],
                                            waits=[tok_acc_ev])
                                    else:
                                        tm_ = pread(
                                            "v", lambda e, ts_=ts_, j=j:
                                            e.tensor_tensor(
                                                tev[:, ts_, :],
                                                py_aps[j][0:128, 0:512],
                                                cBs[:, ts_ * 512:
                                                    (ts_ + 1) * 512],
                                                ALU.mult), py_keys[j],
                                            waits=[tok_cb, tok_acc_ev])
                                        tok_acc_ev = P.op(
                                            "v", lambda e, dt=dt, ts_=ts_:
                                            e.tensor_tensor(
                                                accs[:, dt, ts_ * 512:
                                                     (ts_ + 1) * 512],
                                                accs[:, dt, ts_ * 512:
                                                     (ts_ + 1) * 512],
                                                tev[:, ts_, :], ALU.add), inc=sv)

        if debug:
            dma("y", gdbg_h["l"], l_sb[:], waits=[tok_cT])
            dma("y", gdbg_h["t"], top8[:], waits=[tok_cT])
            dma("y", gdbg_h["n"], negl1[:], waits=[tok_cT])
            dma("y", gdbg_h["p"], peg[:], waits=[tok_cT])
            dma("y", gdbg_h["r"], r12[:], waits=[tok_cT])
            dma("y", fct_h[tau], cT[:], waits=[tok_cT])
            dma("y", facc_h[tau].rearrange("(c p) t -> p c t", p=128),
                accs[:], waits=[tok_acc_ev])
        tok_f3 = fence("v")
        tok_exp_pe = (sp, sp.v)
        tok_x2_rd = None
        for ch in range(2):
            tok_x2 = dma("y", f3x2[:, :, 0:512],
                         x2fm_r[:, :, t0 + ch * 512:t0 + (ch + 1) * 512],
                         waits=[tok_scr_done, tok_exp_pe] +
                               ([tok_x2_rd] if tok_x2_rd else []))
            accv = accs[:, :, ch * 512:(ch + 1) * 512]
            tb1, tb2 = ln_feature(accv, 512, tok_f3, None, None, f3sq, "s",
                                  bfbuf=f3ab)
            tof = None
            for kc in range(KC):
                pread("v", lambda e, kc=kc, ch=ch: e.tensor_tensor(
                    tmpn[:, 0:512], accs[:, kc, ch * 512:(ch + 1) * 512],
                    pC[:, 0:512], ALU.subtract), "C", waits=[tb1, tb2])
                pread("v", lambda e: e.tensor_tensor(
                    tmpn[:, 0:512], tmpn[:, 0:512], pD[:, 0:512], ALU.mult),
                    "D")
                tof = P.op("v", lambda e, kc=kc: e.tensor_tensor(
                    f3of[:, kc, 0:512], tmpn[:, 0:512], f3x2[:, kc, 0:512],
                    ALU.add), waits=[tok_x2], inc=sv)
            tof = fence("v")
            tok_x2_rd = tof
            for tc4 in range(4):
                bi = tc4 % 2
                r0 = t0 + ch * 512 + tc4 * 128
                cp_toks = []
                for half in range(2):
                    key = ["E", "F"][half]
                    pap = [pE, pF][half]
                    for j in range(4):
                        dt = half * 4 + j
                        trans(pap[:, j * 128:(j + 1) * 128],
                              f3of[:, dt, tc4 * 128:(tc4 + 1) * 128],
                              idf_s[:], key, waits=[tof])
                    cp_toks.append(pread(
                        "v", lambda e, bi=bi, half=half, pap=pap:
                        e.tensor_copy(f3tm[:, bi, half * 512:(half + 1) * 512],
                                      pap[:, 0:512]), key,
                        waits=[tok_tm_free[bi]]))
                ftm = fence("v")
                last = dma("y", outsg_h[r0:r0 + 128, :], f3tm[:, bi, :],
                           waits=[ftm])
                out_dma_toks.append(last)
                tok_tm_free[bi] = last
        tok_xn2s_rd = (sp, sp.v)
        tok_f3_act = (ss, ss.v)

    tok_allsg = (sy, sy.v)
    for s in range(3):
        dma("y", out_h[s:3 * (NR[s] - 1) + s + 1:3, :],
            outsg_h[OFF[s]:OFF[s] + NR[s], :], waits=[tok_allsg])

    P.wait("y", (sy, sy.v))
    P.wait("g", (sg, sg.v))

    with nc.allow_low_precision(reason="bf16 matmul operands are sanctioned for this problem"), \
            nc.Block() as block:
        @block.tensor
        def _(t):
            for fn in P.q["t"]:
                fn(t)

        @block.vector
        def _(v):
            for fn in P.q["v"]:
                fn(v)

        @block.scalar
        def _(sc):
            for fn in P.q["s"]:
                fn(sc)

        @block.gpsimd
        def _(g):
            for fn in P.q["g"]:
                fn(g)

        @block.sync
        def _(s_):
            for fn in P.q["y"]:
                fn(s_)
    st.close()
    return nc


def _consts():
    idf = np.eye(128, dtype=np.float32)
    idb = idf.astype(ml_dtypes.bfloat16)
    strip = np.ones((128, 132), np.float32)
    for p in range(128):
        for c in range(p, p + 5):
            if 0 <= c < 132:
                strip[p, c] = 2.0
    fmask = np.ones((128, NB), np.float32)
    q_idx = np.arange(NB)
    for p in range(86):
        k = 1280 + p
        fmask[p] = np.where(np.abs(q_idx - k) <= 2, 2.0, 1.0)
    for p, act in ((86, [1364, 1365]), (87, [1365]), (88, [0, 1]), (89, [0])):
        row = np.zeros(NB, np.float32)
        row[act] = 1.0
        fmask[p] = row
    fmask[90:] = 0.0
    oh = np.zeros((8, 1024), np.float32)
    for e in range(8):
        oh[e, e * 128:(e + 1) * 128] = 1.0
    return (idf, idb, strip.astype(ml_dtypes.bfloat16),
            fmask.astype(ml_dtypes.bfloat16), oh.astype(ml_dtypes.bfloat16))


def kernel(L0_attn_x, L0_attn_W_q, L0_attn_W_k, L0_attn_W_v, L0_ffn_gate_w,
           L0_ffn_W1, L0_ffn_b1, L0_ffn_W2, L0_ffn_b2, L0_ffn_sW1, L0_ffn_sb1,
           L0_ffn_sW2, L0_ffn_sb2):
    f = np.float32
    x = np.ascontiguousarray(np.asarray(L0_attn_x, f))
    idf, idb, strip, fmask, oh = _consts()
    common = {
        "wq": np.ascontiguousarray(np.asarray(L0_attn_W_q, f)),
        "wk": np.ascontiguousarray(np.asarray(L0_attn_W_k, f)),
        "wv": np.ascontiguousarray(np.asarray(L0_attn_W_v, f)),
        "gwt": np.ascontiguousarray(np.asarray(L0_ffn_gate_w, f).T),
        "w1": np.ascontiguousarray(np.asarray(L0_ffn_W1, f)),
        "b1t": np.ascontiguousarray(np.concatenate(
            [np.asarray(L0_ffn_b1, f).T,
             np.asarray(L0_ffn_sb1, f)[:, None]], axis=1)),
        "w2": np.ascontiguousarray(np.asarray(L0_ffn_W2, f)),
        "b2": np.ascontiguousarray(np.asarray(L0_ffn_b2, f)),
        "sw1": np.ascontiguousarray(np.asarray(L0_ffn_sW1, f)),
        "sw2": np.ascontiguousarray(np.asarray(L0_ffn_sW2, f)),
        "sb2": np.ascontiguousarray(np.asarray(L0_ffn_sb2, f)),
        "idf": idf, "idb": idb, "strip": strip, "fmask": fmask, "oh": oh,
    }
    nc = build()
    in_maps = [dict(common, x=np.ascontiguousarray(x[b])) for b in range(8)]
    res = run_bass_kernel_spmd(nc, in_maps, core_ids=list(range(8)))
    return np.stack([np.asarray(res.results[b]["out"], f) for b in range(8)], 0)


# version 06
# speedup vs baseline: 1.1178x; 1.1178x over previous
import math
import numpy as np
import ml_dtypes
import concourse.bass as bass
import concourse.mybir as mybir
from concourse.bass_utils import run_bass_kernel_spmd
from contextlib import ExitStack

F32 = mybir.dt.float32
BF16 = mybir.dt.bfloat16
AF = mybir.ActivationFunctionType
ALU = mybir.AluOpType

SEQ, D, NH = 4096, 1024, 16
NB = 1366
NR = [1366, 1365, 1365]
OFF = [0, 1366, 2731]
NT = 11
NBP = 1408
SHIFT = 8.0
KC = 8
EPS = 1e-5
NE = 9
ENGS = ("t", "v", "s", "g", "y")


def qchunks(n, w=512):
    out, o = [], 0
    while o < n:
        c = min(w, n - o)
        out.append((o, c))
        o += c
    return out


def sg_ranges(r0, r1):
    bounds = [0, 1366, 2731, 4096]
    for s in range(3):
        a, b = max(r0, bounds[s]), min(r1, bounds[s + 1])
        if a < b:
            yield a, b - a, 3 * (a - OFF[s]) + s


class Sem:
    def __init__(self, nc, st, name):
        self.h = st.enter_context(nc.semaphore(name))
        self.v = 0

    def bump(self, n=1):
        self.v += n
        return (self, self.v)


class Prog:
    def __init__(self):
        self.q = {k: [] for k in ENGS}
        self.wm = {k: {} for k in ENGS}

    def op(self, ek, builder, waits=(), inc=None, inc_n=1):
        wm = self.wm[ek]
        ws = []
        for t in waits:
            if t is None:
                continue
            sm, val = t
            if wm.get(sm, 0) < val:
                wm[sm] = val
                ws.append((sm.h, val))
        tok = None
        inc_args = None
        if inc is not None:
            tok = inc.bump(inc_n)
            inc_args = (inc.h, inc_n)
        def fn(eng, b=builder, ws=tuple(ws), ia=inc_args):
            for (h, v) in ws:
                eng.wait_ge(h, v)
            ins = b(eng)
            if ia is not None:
                ins.then_inc(*ia)
        self.q[ek].append(fn)
        return tok

    def wait(self, ek, *tokens):
        wm = self.wm[ek]
        ws = []
        for t in tokens:
            if t is None:
                continue
            sm, val = t
            if wm.get(sm, 0) < val:
                wm[sm] = val
                ws.append((sm.h, val))
        if ws:
            def fn(eng, ws=tuple(ws)):
                for (h, v) in ws:
                    eng.wait_ge(h, v)
            self.q[ek].append(fn)


def build(debug=False):
    nc = bass.Bass("TRN2", target_bir_lowering=False, debug=False)

    def din(name, shape, dt=F32):
        return nc.dram_tensor(name, shape, dt, kind="ExternalInput").ap()

    x_h = din("x", [SEQ, D])
    wq_h = din("wq", [D, D]); wk_h = din("wk", [D, D]); wv_h = din("wv", [D, D])
    gwt_h = din("gwt", [D, 8])
    w1_h = din("w1", [8, D, 4096]); b1t_h = din("b1t", [4096, 9])
    w2_h = din("w2", [8, 4096, D]); b2_h = din("b2", [8, D])
    sw1_h = din("sw1", [D, 4096])
    sw2_h = din("sw2", [4096, D]); sb2_h = din("sb2", [D])
    idf_h = din("idf", [128, 128]); idb_h = din("idb", [128, 128], BF16)
    strip_h = din("strip", [128, 132], BF16)
    fmask_h = din("fmask", [128, NB], BF16)
    oh_h = din("oh", [8, 1024], BF16)
    out_h = nc.dram_tensor("out", [SEQ, D], F32, kind="ExternalOutput").ap()

    skind = "ExternalOutput" if debug else "Internal"
    xfm_h = nc.dram_tensor("xfm", [D, SEQ], F32, kind=skind).ap()
    x2fm_h = nc.dram_tensor("x2fm", [D, SEQ], F32, kind=skind).ap()
    xn2fm_h = nc.dram_tensor("xn2fm", [D, SEQ], F32, kind=skind).ap()
    aout_h = nc.dram_tensor("aout", [3, D, NBP], F32, kind=skind).ap()
    outsg_h = nc.dram_tensor("outsg", [SEQ, D], F32, kind="Internal").ap()
    facc_h = nc.dram_tensor("facc", [4, D, 1024], F32, kind=skind).ap() if debug else None
    gdbg_h = {k: nc.dram_tensor("gd_" + k, [128, w], F32, kind=skind).ap()
              for (k, w) in (("l", 8), ("t", 8), ("n", 1), ("p", 8), ("r", 1))} if debug else None
    fct_h = nc.dram_tensor("fct", [4, 9, 1024], BF16, kind=skind).ap() if debug else None
    gall_h = {k: nc.dram_tensor("ga_" + k, [32, 128, w], F32, kind=skind).ap()
              for (k, w) in (("l", 8), ("t", 8), ("n", 1), ("e", 1), ("r", 1),
                             ("d", 1))} if debug else None
    xfm_r = xfm_h.rearrange("(c p) t -> p c t", p=128)
    x2fm_r = x2fm_h.rearrange("(c p) t -> p c t", p=128)
    xn2fm_r = xn2fm_h.rearrange("(c p) t -> p c t", p=128)

    st = ExitStack()

    base0 = (nc.sbuf_base + 511) // 512 * 512
    cur = [base0]
    def sb(name, shape, dt, at=None):
        nel = 1
        for s_ in shape[1:]:
            nel *= s_
        nb = nel * (2 if dt == BF16 else 4)
        if at is None:
            at = (cur[0] + 63) // 64 * 64
            cur[0] = at + nb
        else:
            assert at % 512 == 0
        return nc.alloc_sbuf_tensor_at(name, shape, dt, offset=at)

    gwT = sb("gwT", [128, 8, 8], F32)
    b2e_f = sb("b2e_f", [9, 1024], F32)
    b2e_b = sb("b2e_b", [9, 1024], BF16)
    b1s = sb("b1s", [128, 288], F32)
    oh_s = sb("oh_s", [8, 1024], BF16)
    idf_s = sb("idf_s", [128, 128], F32)
    idb_s = sb("idb_s", [128, 128], BF16)
    onescol = sb("onescol", [128, 1], BF16)
    onesrow = sb("onesrow", [1, 128], BF16)
    ones64 = sb("ones64", [1, 64], BF16)
    strip_s = sb("strip_s", [128, 132], BF16)
    fmask_s = sb("fmask_s", [128, NB], BF16)
    cT = sb("cT", [9, 1024], BF16)
    mean_r = sb("mean_r", [1, 512], F32)
    var_r = sb("var_r", [1, 512], F32)
    rstd_r = sb("rstd_r", [1, 512], F32)
    tmq_r = sb("tmq_r", [1, 512], F32)
    mbf_r = sb("mbf_r", [1, 512], BF16)
    rbf_r = sb("rbf_r", [1, 512], BF16)
    rz_b = sb("rz_b", [1, 1536], BF16)
    l_sb = sb("l_sb", [128, 8], F32)
    top8 = sb("top8", [128, 8], F32)
    negl1 = sb("negl1", [128, 1], F32)
    d12v = sb("d12v", [128, 1], F32)
    e2v = sb("e2v", [128, 1], F32)
    r12 = sb("r12", [128, 1], F32)
    selg = sb("selg", [128, 8], F32)
    peg = sb("peg", [128, 8], F32)
    t1g = sb("t1g", [128, 8], F32)
    cg = sb("cg", [128, 8], F32)
    constv = sb("constv", [128, 2], F32)
    fbv = sb("fbv", [1, 2], F32)
    fbs = sb("fbs", [1, 2], F32)

    AB = (cur[0] + 511) // 512 * 512

    wq_b = sb("wq_b", [128, 8, 1024], BF16, at=AB)
    wk_b = sb("wk_b", [128, 8, 1024], BF16, at=AB + 16384)
    wv_b = sb("wv_b", [128, 8, 1024], BF16, at=AB + 32768)
    q_fm = sb("q_fm", [128, 8, NBP], BF16, at=AB + 49152)
    k_fm = sb("k_fm", [128, 8, NBP], BF16, at=AB + 71680)
    v_ex = sb("v_ex", [128, NT, 1040], BF16, at=AB + 94208)
    xn1 = sb("xn1", [128, 8, NBP], BF16, at=AB + 117248)
    xch = sb("xch", [128, 8, 512], F32, at=AB + 139776)
    xbch = sb("xbch", [128, 8, 512], BF16, at=AB + 156160)
    xsq = sb("xsq", [128, 8, 512], BF16, at=AB + 164352)
    ptb = sb("ptb", [128, 3, NBP], BF16, at=AB + 156160)
    aost = sb("aost", [128, NBP], F32, at=AB + 164864)
    tmpn = sb("tmpn", [128, 512], F32, at=AB + 172544)
    xin = sb("xin", [128, 2, 2, 1024], F32, at=AB + 139776)
    a0st = sb("a0st", [128, 2, 2048], F32, at=AB + 156160)
    ach = sb("ach", [128, 8, 512], F32, at=AB + 49152)
    x2ch = sb("x2ch", [128, 8, 512], F32, at=AB + 65536)
    xn2ch = sb("xn2ch", [128, 8, 512], F32, at=AB + 81920)
    x2bf = sb("x2bf", [128, 8, 512], BF16, at=AB + 98304)
    abf = sb("abf", [128, 8, 512], BF16, at=AB + 106496)
    asq = sb("asq", [128, 8, 512], BF16, at=AB + 114688)
    xn2s = sb("xn2s", [128, 8, 1024], BF16, at=AB)
    accs = sb("accs", [128, 8, 1024], F32, at=AB + 16384)
    h_sb = sb("h_sb", [128, 32, 1024], BF16, at=AB + 49152)
    w1c = sb("w1c", [128, 2, 8, 512], BF16, at=AB + 114688)
    w2c = sb("w2c", [128, 2, 4, 512], BF16, at=AB + 131072)
    cBs = sb("cBs", [128, 1024], F32, at=AB + 139264)
    tev = sb("tev", [128, 2, 512], F32, at=AB + 143360)
    g32 = sb("g32", [128, 2, 8, 128], F32, at=AB + 147456)
    f3x2 = sb("f3x2", [128, 8, 512], F32, at=AB + 49152)
    f3of = sb("f3of", [128, 8, 512], F32, at=AB + 65536)
    f3ab = sb("f3ab", [128, 8, 512], BF16, at=AB + 81920)
    f3sq = sb("f3sq", [128, 8, 512], BF16, at=AB + 90112)
    f3tm = sb("f3tm", [128, 2, 1024], F32, at=AB + 98304)

    assert AB + 174592 <= nc.sbuf_top, f"sbuf overflow AB={AB} top={nc.sbuf_top}"

    pA = st.enter_context(nc.psum_tensor("pA", [128, 512], F32))
    pB = st.enter_context(nc.psum_tensor("pB", [128, 512], F32))
    pC = st.enter_context(nc.psum_tensor("pC", [128, 512], F32))
    pD = st.enter_context(nc.psum_tensor("pD", [128, 512], F32))
    pE = st.enter_context(nc.psum_tensor("pE", [128, 512], F32))
    pF = st.enter_context(nc.psum_tensor("pF", [128, 512], F32))
    pW = st.enter_context(nc.psum_tensor("pW", [128, 1024], F32))

    P = Prog()
    sg = Sem(nc, st, "sg")
    sy = Sem(nc, st, "sy")
    sp = Sem(nc, st, "sp")
    sv = Sem(nc, st, "sv")
    ss = Sem(nc, st, "ss")

    def fence(ek):
        if ek == "v":
            return P.op("v", lambda e: e.memset(fbv[0:1, 0:1], 0.0), inc=sv)
        return P.op("s", lambda e: e.memzero(fbs[0:1, 0:1]), inc=ss)

    BG = {}
    def bank_w_waits(key):
        e = BG.get(key)
        return list(e["rs"]) if e else []
    def bank_note_w(key, tok):
        BG[key] = {"w": tok, "rs": []}
    def bank_r_wait(key):
        e = BG.get(key)
        return e["w"] if e else None
    def bank_note_r(key, tok):
        BG[key]["rs"].append(tok)

    def dma(qk, dst, src, waits=()):
        semq = sg if qk == "g" else sy
        return P.op(qk, lambda e, d=dst, s_=src: e.dma_start(d, s_),
                    waits=waits, inc=semq, inc_n=16)

    def mm(lhsT, rhs, out, key, start, stop, waits=(), force_inc=False):
        w = list(waits)
        if start:
            w += bank_w_waits(key)
        do_inc = stop or force_inc
        tok = P.op("t", lambda e, a=lhsT, b=rhs, o=out, s1=start, s2=stop:
                   e.matmul(o, a, b, start=s1, stop=s2),
                   waits=w, inc=(sp if do_inc else None))
        if stop:
            bank_note_w(key, tok)
        return tok

    def trans(out, in_, ident, key, waits=()):
        w = list(waits) + bank_w_waits(key)
        tok = P.op("t", lambda e, o=out, i=in_, d=ident:
                   e.transpose(o, i, d), waits=w, inc=sp)
        bank_note_w(key, tok)
        return tok

    def pread(ek, builder, key, waits=(), inc=None):
        w = list(waits)
        bw = bank_r_wait(key)
        if bw is not None:
            w.append(bw)
        tok = P.op(ek, builder, waits=w, inc=(inc or (sv if ek == "v" else ss)))
        bank_note_r(key, tok)
        return tok

    tok_idf = dma("y", idf_s[:], idf_h)
    tok_idb = dma("y", idb_s[:], idb_h)
    tok_strip = dma("y", strip_s[:], strip_h)
    tok_fmask = dma("y", fmask_s[:], fmask_h)
    tok_oh = dma("y", oh_s[:], oh_h)
    tok_b2 = dma("y", b2e_f[0:8, :], b2_h)
    tok_sb2 = dma("y", b2e_f[8:9, :], sb2_h.rearrange("(a d) -> a d", a=1))
    tok_gwT = dma("y", gwT[:], gwt_h.rearrange("(c p) e -> p c e", p=128))
    tok_b1s = dma("y", b1s[:].rearrange("p (f e) -> p f e", e=9),
                  b1t_h.rearrange("(f p) e -> p f e", p=128))
    tok_wq = dma("g", wq_b[:], wq_h.rearrange("(c p) o -> p c o", p=128))
    tok_wk = dma("g", wk_b[:], wk_h.rearrange("(c p) o -> p c o", p=128))
    tok_wv = dma("g", wv_b[:], wv_h.rearrange("(c p) o -> p c o", p=128))

    P.op("s", lambda e: e.copy(b2e_b[:], b2e_f[:]),
         waits=[tok_b2, tok_sb2], inc=ss)
    P.op("v", lambda e: e.memset(constv[:, 0:1], EPS))
    P.op("v", lambda e: e.memset(constv[:, 1:2], -SHIFT))
    P.op("v", lambda e: e.memset(cT[:], 1.0))
    P.op("v", lambda e: e.memset(onescol[:], 1.0))
    P.op("v", lambda e: e.memset(onesrow[:], 1.0))
    P.op("v", lambda e: e.memset(ones64[:], 1.0), inc=sv)
    tok_ones = fence("v")
    tok_b2c = fence("s")

    tr_keys = ["W0", "W1"]
    tr_aps = [pW[:, 0:512], pW[:, 512:1024]]
    tok_ld = [None, None]
    tok_st_dma = [None, None]
    for ci in range(16):
        r0 = ci * 256
        bi = ci % 2
        w0 = [tok_st_dma[bi]]
        for th in range(2):
            rb = r0 + th * 128
            for (a, n, t0) in sg_ranges(rb, rb + 128):
                tok_ld[bi] = dma("y", xin[a - rb:a - rb + n, bi, th, :],
                                 x_h[t0:t0 + 3 * (n - 1) + 1:3, :], waits=w0)
                w0 = []
        a0v = a0st[:, bi, :].rearrange("p (c t) -> p c t", c=8)
        for th in range(2):
            for half in range(2):
                for j in range(4):
                    dt = half * 4 + j
                    trans(tr_aps[half][:, j * 128:(j + 1) * 128],
                          xin[:, bi, th, dt * 128:(dt + 1) * 128],
                          idf_s[:], tr_keys[half], waits=[tok_ld[bi], tok_idf])
                pread("v", lambda e, a0v=a0v, half=half, th=th: e.tensor_copy(
                    a0v[:, half * 4:(half + 1) * 4, th * 128:(th + 1) * 128],
                    tr_aps[half].rearrange("p (a b) -> p a b", a=4)),
                    tr_keys[half], waits=[tok_st_dma[bi]])
        fa0 = fence("v")
        tok_st_dma[bi] = dma("y", xfm_r[:, :, r0:r0 + 256], a0v, waits=[fa0])
    tok_xfm = (sy, sy.v)

    def ln_feature(xf32, qw, tok_x, bfsrc, tok_bf, sqbuf, cast_eng,
                   bfbuf=None):
        tsq, tcast = [], []
        for kc in range(KC):
            tsq.append(P.op("s", lambda e, kc=kc, qw=qw: e.activation(
                sqbuf[:, kc, 0:qw], xf32[:, kc, 0:qw], AF.Square),
                waits=[tok_x], inc=ss))
        if bfsrc is None:
            bfsrc = bfbuf
            for kc in range(KC):
                tcast.append(P.op(cast_eng, lambda e, kc=kc, qw=qw: (
                    e.copy(bfbuf[:, kc, 0:qw], xf32[:, kc, 0:qw])
                    if cast_eng == "s" else
                    e.tensor_copy(bfbuf[:, kc, 0:qw], xf32[:, kc, 0:qw])),
                    waits=[tok_x], inc=(ss if cast_eng == "s" else sv)))
        fs_pre = fence("s")
        tok_sum = tok_sumsq = None
        for kc in range(KC):
            w = [fs_pre if (tcast and kc == KC - 1) else
                 (tcast[min(kc + 1, KC - 1)] if tcast else tok_bf)]
            tok_sum = mm(onescol[:], bfsrc[:, kc, 0:qw], pA[0:1, 0:qw], "A",
                         kc == 0, kc == KC - 1, waits=w)
        for kc in range(KC):
            w = [fs_pre if kc == KC - 1 else tsq[kc + 1]]
            tok_sumsq = mm(onescol[:], sqbuf[:, kc, 0:qw], pB[0:1, 0:qw], "B",
                           kc == 0, kc == KC - 1, waits=w)
        tmean = pread("v", lambda e, qw=qw: e.tensor_scalar(
            mean_r[0:1, 0:qw], pA[0:1, 0:qw], 1.0 / D, None, ALU.mult),
            "A", waits=[tok_sum])
        tsq_ = P.op("v", lambda e, qw=qw: e.tensor_tensor(
            tmq_r[0:1, 0:qw], mean_r[0:1, 0:qw], mean_r[0:1, 0:qw], ALU.mult),
            waits=[tmean], inc=sv)
        tvar = pread("v", lambda e, qw=qw: e.scalar_tensor_tensor(
            var_r[0:1, 0:qw], pB[0:1, 0:qw], 1.0 / D, tmq_r[0:1, 0:qw],
            ALU.mult, ALU.subtract), "B", waits=[tok_sumsq, tsq_])
        fvv = fence("v")
        tln = P.op("s", lambda e, qw=qw: e.activation(
            tmq_r[0:1, 0:qw], var_r[0:1, 0:qw], AF.Ln,
            bias=constv[0:1, 0:1]), waits=[fvv, tok_ones], inc=ss)
        trs = P.op("s", lambda e, qw=qw: e.activation(
            rstd_r[0:1, 0:qw], tmq_r[0:1, 0:qw], AF.Exp, scale=-0.5), inc=ss)
        fss = fence("s")
        tmb = P.op("v", lambda e, qw=qw: e.tensor_copy(
            mbf_r[0:1, 0:qw], mean_r[0:1, 0:qw]), waits=[tmean], inc=sv)
        trb = P.op("v", lambda e, qw=qw: e.tensor_copy(
            rbf_r[0:1, 0:qw], rstd_r[0:1, 0:qw]), waits=[fss], inc=sv)
        fvb = fence("v")
        tb1 = mm(onesrow[0:1, 0:128], mbf_r[0:1, 0:qw], pC[:, 0:qw], "C",
                 True, True, waits=[fvb, tok_ones])
        tb2 = mm(onesrow[0:1, 0:128], rbf_r[0:1, 0:qw], pD[:, 0:qw], "D",
                 True, True, waits=[fvb])
        return tb1, tb2

    tok_xn1_rd = None
    tok_vex_rd = None
    tok_stage = [None, None]
    tok_pv = {}
    tok_bc = {}
    tok_a3_last = None

    for s in range(3):
        ncol = NR[s]
        qch_real = qchunks(ncol)
        qch_full = qchunks(NB)

        tok_prev_sy = (sy, sy.v) if sy.v else None
        wpad = [tok_xn1_rd, tok_prev_sy]
        tok_pad_xn1 = P.op("v", lambda e, n_=ncol: e.memset(
            xn1[:, :, n_:NBP], 0.0), waits=wpad, inc=sv)
        tok_pad_q = P.op("v", lambda e, n_=ncol: e.memset(
            q_fm[:, :, n_:NBP], 0.0), waits=wpad, inc=sv)
        tok_pad_k = P.op("v", lambda e, n_=ncol: e.memset(
            k_fm[:, :, n_:NBP], 0.0), waits=wpad, inc=sv)
        tok_vex_ones = P.op("v", lambda e: e.memset(v_ex[:], 1.0),
                            waits=[tok_vex_rd, tok_prev_sy], inc=sv)

        tok_norm_last = tok_a3_last if s > 0 else None
        for (qo, qw) in qch_real:
            col0 = OFF[s] + qo
            w0 = [tok_xfm, tok_xn1_rd, tok_stage[0], tok_stage[1],
                  tok_prev_sy] + ([tok_norm_last] if tok_norm_last else [])
            tok_xc = dma("y", xch[:, :, 0:qw], xfm_r[:, :, col0:col0 + qw],
                         waits=w0)
            tok_xb = dma("g", xbch[:, :, 0:qw], xfm_r[:, :, col0:col0 + qw],
                         waits=w0)
            tb1, tb2 = ln_feature(xch, qw, tok_xc, xbch, tok_xb, xsq, None)
            for kc in range(KC):
                pread("v", lambda e, kc=kc, qw=qw: e.tensor_tensor(
                    tmpn[:, 0:qw], xch[:, kc, 0:qw], pC[:, 0:qw],
                    ALU.subtract), "C", waits=[tb1, tb2, tok_pad_xn1])
                tok_norm_last = pread("v", lambda e, kc=kc, qw=qw, qo=qo:
                                      e.tensor_tensor(
                                      xn1[:, kc, qo:qo + qw], tmpn[:, 0:qw],
                                      pD[:, 0:qw], ALU.mult), "D")
        tok_xn1_w = fence("v")

        pq_keys = ["E", "F"]
        pq_aps = [pE, pF]
        qi = 0
        tok_qk_last = None
        for (dst, wsrc, tok_w, tpad) in ((q_fm, wq_b, tok_wq, tok_pad_q),
                                         (k_fm, wk_b, tok_wk, tok_pad_k)):
            for dt in range(8):
                for (qo, qw) in qch_full:
                    key = pq_keys[qi % 2]
                    bank = pq_aps[qi % 2]
                    tstop = None
                    for kc in range(KC):
                        tstop = mm(wsrc[:, kc, dt * 128:(dt + 1) * 128],
                                   xn1[:, kc, qo:qo + qw], bank[:, 0:qw],
                                   key, kc == 0, kc == KC - 1,
                                   waits=[tok_xn1_w, tok_w])
                    tok_qk_last = pread("v", lambda e, dst=dst, dt=dt, qo=qo,
                                        qw=qw, bank=bank: e.tensor_copy(
                                        dst[:, dt, qo:qo + qw], bank[:, 0:qw]),
                                        key, waits=[tstop, tpad])
                    qi += 1
        tok_qk_ev = fence("v")
        tok_v_ev = None
        for tc in range(NT):
            tstop = None
            for kc in range(KC):
                for vh in range(2):
                    key = tr_keys[vh]
                    tstop = mm(xn1[:, kc, tc * 128:(tc + 1) * 128],
                               wv_b[:, kc, vh * 512:(vh + 1) * 512],
                               pW[:, vh * 512:(vh + 1) * 512], key,
                               kc == 0, kc == KC - 1,
                               waits=[tok_xn1_w, tok_wv])
            for vh in range(2):
                tok_v_ev = pread(
                    "v", lambda e, tc=tc, vh=vh: e.tensor_copy(
                        v_ex[:, tc, vh * 520:vh * 520 + 520].rearrange(
                            "p (h c) -> p h c", h=8)[:, :, 0:64],
                        pW[:, vh * 512:(vh + 1) * 512].rearrange(
                            "p (h c) -> p h c", h=8)),
                    tr_keys[vh], waits=[tstop, tok_vex_ones])
        tok_v_ev = fence("v")

        st_keys = ["A", "B", "F", "W0"]
        st_aps = [pA, pB, pF, pW[:, 0:512]]
        o_keys = ["C", "D", "E"]
        o_aps = [pC, pD, pE]
        rz_keys = ["A", "B", "W1"]
        rz_aps = [pA[0:64, :], pB[0:64, :], pW[0:64, 512:1024]]
        LAG = 2
        for h in range(NH):
            dtile, p0 = h // 2, (h % 2) * 64
            tok_fence = {}
            for kcx in range(NT + LAG):
                if kcx < NT:
                    kc = kcx
                    par = kc % 3
                    for (ci_, (qo, qw)) in enumerate(qch_full):
                        i = kc * 3 + ci_
                        key = st_keys[i % 4]
                        bank = st_aps[i % 4]
                        tst = mm(k_fm[p0:p0 + 64, dtile,
                                      kc * 128:(kc + 1) * 128],
                                 q_fm[p0:p0 + 64, dtile, qo:qo + qw],
                                 bank[0:128, 0:qw], key, True, True,
                                 waits=[tok_qk_ev])
                        wpt = [tok_pv.get((par, ci_))]
                        texp = pread("s", lambda e, par=par, qo=qo, qw=qw,
                                     bank=bank: e.activation(
                                     ptb[:, par, qo:qo + qw],
                                     bank[0:128, 0:qw],
                                     AF.Exp, bias=constv[:, 1:2], scale=0.125),
                                     key, waits=wpt + [tok_ones])
                        if kc < 10:
                            kb = kc * 128
                            a_, b_ = max(qo, kb - 2), min(qo + qw, kb + 130)
                            if a_ < b_:
                                P.op("v", lambda e, par=par, a_=a_,
                                     b_=b_, kb=kb: e.tensor_tensor(
                                     ptb[:, par, a_:b_],
                                     ptb[:, par, a_:b_],
                                     strip_s[:, a_ - kb + 2:b_ - kb + 2],
                                     ALU.mult),
                                     waits=[texp, tok_strip], inc=sv)
                        else:
                            P.op("v", lambda e, par=par, qo=qo, qw=qw:
                                 e.tensor_tensor(
                                 ptb[:, par, qo:qo + qw],
                                 ptb[:, par, qo:qo + qw],
                                 fmask_s[:, qo:qo + qw], ALU.mult),
                                 waits=[texp, tok_fmask], inc=sv)
                    tok_fence[kc] = (fence("v"), fence("s"))
                if kcx >= LAG:
                    kc = kcx - LAG
                    par = kc % 3
                    fkv, fks = tok_fence.pop(kc)
                    for (ci_, (qo, qw)) in enumerate(qch_full):
                        tok_pv[(par, ci_)] = mm(
                            v_ex[:, kc, h * 65:(h + 1) * 65],
                            ptb[:, par, qo:qo + qw],
                            o_aps[ci_][0:65, 0:qw], o_keys[ci_],
                            kc == 0, kc == NT - 1,
                            waits=[fkv, fks, tok_v_ev],
                            force_inc=True)
            ev_toks = []
            for (ci_, (qo, qw)) in enumerate(qch_full):
                trz = pread("v", lambda e, ci_=ci_, qw=qw: e.reciprocal(
                    rz_b[0:1, ci_ * 512:ci_ * 512 + qw],
                    o_aps[ci_][64:65, 0:qw]), o_keys[ci_],
                    waits=[tok_bc.get(ci_)])
                frz = fence("v")
                tbc = mm(ones64[0:1, 0:64],
                         rz_b[0:1, ci_ * 512:ci_ * 512 + qw],
                         rz_aps[ci_][:, 0:qw], rz_keys[ci_], True, True,
                         waits=[frz, tok_ones])
                tok_bc[ci_] = tbc
                trc = pread("v", lambda e, ci_=ci_, qw=qw: e.tensor_copy(
                    tmpn[0:64, 0:qw], rz_aps[ci_][:, 0:qw]), rz_keys[ci_],
                    waits=[tbc])
                tev_ = pread("v", lambda e, ci_=ci_, qo=qo, qw=qw, p0=p0:
                             e.tensor_tensor(aost[p0:p0 + 64, qo:qo + qw],
                                             o_aps[ci_][0:64, 0:qw],
                                             tmpn[0:64, 0:qw], ALU.mult),
                             o_keys[ci_], waits=[tok_stage[h % 2], trc])
                ev_toks.append(tev_)
            fev = fence("v")
            tok_stage[h % 2] = dma("y", aout_h[s, h * 64:(h + 1) * 64, 0:NB],
                                   aost[p0:p0 + 64, 0:NB], waits=[fev])
        tok_xn1_rd = (sp, sp.v)
        tok_vex_rd = (sp, sp.v)
        tok_aout_s = [tok_stage[0], tok_stage[1]]

        tok_a3_last = None
        for (qo, qw) in qch_real:
            col0 = OFF[s] + qo
            w0 = [t for t in tok_aout_s if t] + \
                 ([tok_a3_last] if tok_a3_last else [])
            tok_ac = dma("y", ach[:, :, 0:qw],
                         aout_h[s].rearrange("(c p) t -> p c t", p=128)[
                             :, :, qo:qo + qw], waits=w0)
            tok_xc = dma("y", xch[:, :, 0:qw], xfm_r[:, :, col0:col0 + qw],
                         waits=w0)
            tb1, tb2 = ln_feature(ach, qw, tok_ac, None, None, asq, "s",
                                  bfbuf=abf)
            tx2 = None
            for kc in range(KC):
                pread("v", lambda e, kc=kc, qw=qw: e.tensor_tensor(
                    tmpn[:, 0:qw], ach[:, kc, 0:qw], pC[:, 0:qw],
                    ALU.subtract), "C", waits=[tb1, tb2])
                pread("v", lambda e, qw=qw: e.tensor_tensor(
                    tmpn[:, 0:qw], tmpn[:, 0:qw], pD[:, 0:qw], ALU.mult), "D")
                tx2 = P.op("v", lambda e, kc=kc, qw=qw: e.tensor_tensor(
                    x2ch[:, kc, 0:qw], tmpn[:, 0:qw], xch[:, kc, 0:qw],
                    ALU.add), waits=[tok_xc], inc=sv)
            tx2 = fence("v")
            dma("y", x2fm_r[:, :, col0:col0 + qw], x2ch[:, :, 0:qw],
                waits=[tx2])
            tb1, tb2 = ln_feature(x2ch, qw, tx2, None, None, asq, "s",
                                  bfbuf=x2bf)
            txn = None
            for kc in range(KC):
                pread("v", lambda e, kc=kc, qw=qw: e.tensor_tensor(
                    tmpn[:, 0:qw], x2ch[:, kc, 0:qw], pC[:, 0:qw],
                    ALU.subtract), "C", waits=[tb1, tb2])
                txn = pread("v", lambda e, kc=kc, qw=qw: e.tensor_tensor(
                    xn2ch[:, kc, 0:qw], tmpn[:, 0:qw], pD[:, 0:qw],
                    ALU.mult), "D")
            txn = fence("v")
            dma("y", xn2fm_r[:, :, col0:col0 + qw], xn2ch[:, :, 0:qw],
                waits=[txn])
            tok_a3_last = txn
    tok_scr_done = (sy, sy.v)
    tok_attn_pe = (sp, sp.v)

    py_keys = ["A", "B", "C", "D", "E", "F", "W0", "W1"]
    py_aps = [pA, pB, pC, pD, pE, pF, pW[:, 0:512], pW[:, 512:1024]]
    tok_xn2s_rd = None
    tok_gate_act = None
    tok_f3_act = None
    tok_tm_free = [None, None]
    out_dma_toks = []
    tok_w1_rd = [[tok_attn_pe, tok_scr_done], [tok_attn_pe, tok_scr_done]]
    tok_w2_rd = [[tok_attn_pe, tok_scr_done], [tok_attn_pe, tok_scr_done]]
    for tau in range(4):
        t0 = tau * 1024
        w0 = [tok_scr_done] + ([tok_xn2s_rd] if tok_xn2s_rd else [])
        tok_xn2 = dma("g", xn2s[:], xn2fm_r[:, :, t0:t0 + 1024], waits=w0)

        tok_ct = []
        tok_g32_rd = [None, None]
        tok_ttr_prev = None
        for tc in range(8):
            gb = tc % 2
            tok_g32 = dma("y", g32[:, gb],
                          xn2fm_r[:, :, t0 + tc * 128:t0 + (tc + 1) * 128],
                          waits=[tok_scr_done, tok_g32_rd[gb]])
            tstop = None
            for kc in range(KC):
                tstop = mm(g32[:, gb, kc, :],
                           gwT[:, kc, :], pA[0:128, 0:8], "A",
                           kc == 0, kc == KC - 1,
                           waits=[tok_g32, tok_gwT])
            tok_g32_rd[gb] = tstop
            tcp = pread("v", lambda e: e.tensor_copy(l_sb[:], pA[0:128, 0:8]),
                        "A", waits=[tstop, tok_gate_act])
            tmax = P.op("v", lambda e: e.max(top8[:], l_sb[:]),
                        waits=[tcp], inc=sv)
            td12 = P.op("v", lambda e: e.tensor_tensor(
                d12v[:], top8[:, 1:2], top8[:, 0:1], ALU.subtract),
                waits=[tmax], inc=sv)
            tneg = P.op("v", lambda e: e.tensor_scalar(
                negl1[:], top8[:, 0:1], -1.0, None, ALU.mult),
                waits=[tok_gate_act, tmax], inc=sv)
            fg1 = fence("v")
            te2 = P.op("s", lambda e: e.activation(e2v[:], d12v[:], AF.Exp),
                       waits=[fg1], inc=ss)
            tpe = P.op("s", lambda e: e.activation(
                peg[:], l_sb[:], AF.Exp, bias=negl1[:]), inc=ss)
            fg2 = fence("s")
            tden = P.op("v", lambda e: e.tensor_scalar(
                d12v[:], e2v[:], 1.0, None, ALU.add), waits=[fg2], inc=sv)
            tr12 = P.op("v", lambda e: e.reciprocal(r12[:], d12v[:]),
                        waits=[tden], inc=sv)
            tsel = P.op("v", lambda e: e.tensor_scalar(
                selg[:], l_sb[:], top8[:, 1:2], None, ALU.is_ge),
                waits=[tmax, tcp], inc=sv)
            tt1 = P.op("v", lambda e: e.tensor_tensor(
                t1g[:], peg[:], selg[:], ALU.mult), waits=[fg2, tsel], inc=sv)
            tc_ = P.op("v", lambda e: e.tensor_scalar(
                cg[:], t1g[:], r12[:], None, ALU.mult),
                waits=[tr12, tt1, tok_ttr_prev], inc=sv)
            fg3 = fence("v")
            ttr = mm(cg[:], idf_s[:], pB[0:8, 0:128], "B", True, True,
                     waits=[fg3, tok_idf])
            tok_ttr_prev = ttr
            tok_ct.append(pread("v", lambda e, tc=tc: e.tensor_copy(
                cT[0:8, tc * 128:(tc + 1) * 128], pB[0:8, 0:128]), "B",
                waits=[ttr]))
            if debug:
                ci = tau * 8 + tc
                dma("y", gall_h["l"][ci], l_sb[:], waits=[fg3])
                dma("y", gall_h["t"][ci], top8[:], waits=[fg3])
                dma("y", gall_h["n"][ci], negl1[:], waits=[fg3])
                dma("y", gall_h["e"][ci], e2v[:], waits=[fg3])
                dma("y", gall_h["r"][ci], r12[:], waits=[fg3])
                dma("y", gall_h["d"][ci], d12v[:], waits=[fg3])
            tok_gate_act = tpe
        tok_cT = fence("v")

        tok_acc_init = None
        for dt in range(8):
            for tsx in range(2):
                tstop = mm(b2e_b[0:9, dt * 128:(dt + 1) * 128],
                           cT[0:9, tsx * 512:(tsx + 1) * 512],
                           pC[:, 0:512], "C", True, True,
                           waits=[tok_cT, tok_b2c])
                tok_acc_init = pread("v", lambda e, dt=dt, tsx=tsx:
                                     e.tensor_copy(
                                     accs[:, dt, tsx * 512:(tsx + 1) * 512],
                                     pC[:, 0:512]), "C",
                                     waits=[tstop, tok_f3_act])

        P.wait("s", (sy, sy.v))
        tok_acc_ev = tok_acc_init
        for e_ in range(NE):
            shared = (e_ == 8)
            w1src = (sw1_h if shared else w1_h[e_])
            w2src = (sw2_h if shared else w2_h[e_])
            tok_cb = None
            if not shared:
                for tsx in range(2):
                    tstop = mm(oh_s[0:8, e_ * 128:(e_ + 1) * 128],
                               cT[0:8, tsx * 512:(tsx + 1) * 512],
                               pD[:, 0:512], "D", True, True,
                               waits=[tok_cT, tok_oh])
                    tok_cb = pread("v", lambda e, tsx=tsx: e.tensor_copy(
                        cBs[:, tsx * 512:(tsx + 1) * 512], pD[:, 0:512]),
                        "D", waits=[tstop])
            tok_gelu = None
            for fc in range(8):
                buf = fc % 2
                tok_w1 = dma("g", w1c[:, buf],
                             w1src[:, fc * 512:(fc + 1) * 512].rearrange(
                                 "(c p) f -> p c f", p=128),
                             waits=tok_w1_rd[buf])
                for fm in range(4):
                    ft = fc * 4 + fm
                    for ts_ in range(2):
                        key = py_keys[ts_]
                        tstop = None
                        for kc in range(KC):
                            tstop = mm(w1c[:, buf, kc, fm * 128:(fm + 1) * 128],
                                       xn2s[:, kc, ts_ * 512:(ts_ + 1) * 512],
                                       py_aps[ts_][0:128, 0:512], key,
                                       kc == 0, kc == KC - 1,
                                       waits=[tok_w1, tok_xn2])
                        if fm == 3 and ts_ == 1:
                            tok_w1_rd[buf] = [tstop]
                        bcol = ft * 9 + (8 if shared else e_)
                        tok_gelu = pread(
                            "s", lambda e, ft=ft, ts_=ts_, bcol=bcol:
                            e.activation(
                                h_sb[:, ft, ts_ * 512:(ts_ + 1) * 512],
                                py_aps[ts_][0:128, 0:512], AF.Gelu,
                                bias=b1s[:, bcol:bcol + 1]), key,
                            waits=[tok_b1s])
            tok_gelu = fence("s")
            for dh in range(2):
                for fc2 in range(8):
                    buf = fc2 % 2
                    tok_w2 = dma("g", w2c[:, buf],
                                 w2src[fc2 * 512:(fc2 + 1) * 512,
                                       dh * 512:(dh + 1) * 512].rearrange(
                                     "(f p) d -> p f d", p=128),
                                 waits=tok_w2_rd[buf])
                    for fs in range(4):
                        ft = fc2 * 4 + fs
                        for dt2 in range(4):
                            for ts_ in range(2):
                                j = dt2 * 2 + ts_
                                last_of_chunk = (fs == 3 and dt2 == 3
                                                 and ts_ == 1)
                                tstop = mm(
                                    w2c[:, buf, fs, dt2 * 128:(dt2 + 1) * 128],
                                    h_sb[:, ft, ts_ * 512:(ts_ + 1) * 512],
                                    py_aps[j][0:128, 0:512], py_keys[j],
                                    fc2 == 0 and fs == 0,
                                    fc2 == 7 and fs == 3,
                                    waits=[tok_w2, tok_gelu],
                                    force_inc=last_of_chunk)
                                if last_of_chunk:
                                    tok_w2_rd[buf] = [tstop]
                                if fc2 == 7 and fs == 3:
                                    dt = dh * 4 + dt2
                                    if shared:
                                        tok_acc_ev = pread(
                                            "v", lambda e, dt=dt, ts_=ts_, j=j:
                                            e.tensor_tensor(
                                                accs[:, dt, ts_ * 512:
                                                     (ts_ + 1) * 512],
                                                accs[:, dt, ts_ * 512:
                                                     (ts_ + 1) * 512],
                                                py_aps[j][0:128, 0:512],
                                                ALU.add), py_keys[j],
                                            waits=[tok_acc_ev])
                                    else:
                                        tm_ = pread(
                                            "v", lambda e, ts_=ts_, j=j:
                                            e.tensor_tensor(
                                                tev[:, ts_, :],
                                                py_aps[j][0:128, 0:512],
                                                cBs[:, ts_ * 512:
                                                    (ts_ + 1) * 512],
                                                ALU.mult), py_keys[j],
                                            waits=[tok_cb, tok_acc_ev])
                                        tok_acc_ev = P.op(
                                            "v", lambda e, dt=dt, ts_=ts_:
                                            e.tensor_tensor(
                                                accs[:, dt, ts_ * 512:
                                                     (ts_ + 1) * 512],
                                                accs[:, dt, ts_ * 512:
                                                     (ts_ + 1) * 512],
                                                tev[:, ts_, :], ALU.add), inc=sv)

        if debug:
            dma("y", gdbg_h["l"], l_sb[:], waits=[tok_cT])
            dma("y", gdbg_h["t"], top8[:], waits=[tok_cT])
            dma("y", gdbg_h["n"], negl1[:], waits=[tok_cT])
            dma("y", gdbg_h["p"], peg[:], waits=[tok_cT])
            dma("y", gdbg_h["r"], r12[:], waits=[tok_cT])
            dma("y", fct_h[tau], cT[:], waits=[tok_cT])
            dma("y", facc_h[tau].rearrange("(c p) t -> p c t", p=128),
                accs[:], waits=[tok_acc_ev])
        tok_f3 = fence("v")
        tok_exp_pe = (sp, sp.v)
        tok_x2_rd = None
        for ch in range(2):
            tok_x2 = dma("y", f3x2[:, :, 0:512],
                         x2fm_r[:, :, t0 + ch * 512:t0 + (ch + 1) * 512],
                         waits=[tok_scr_done, tok_exp_pe] +
                               ([tok_x2_rd] if tok_x2_rd else []))
            accv = accs[:, :, ch * 512:(ch + 1) * 512]
            tb1, tb2 = ln_feature(accv, 512, tok_f3, None, None, f3sq, "s",
                                  bfbuf=f3ab)
            tof = None
            for kc in range(KC):
                pread("v", lambda e, kc=kc, ch=ch: e.tensor_tensor(
                    tmpn[:, 0:512], accs[:, kc, ch * 512:(ch + 1) * 512],
                    pC[:, 0:512], ALU.subtract), "C", waits=[tb1, tb2])
                pread("v", lambda e: e.tensor_tensor(
                    tmpn[:, 0:512], tmpn[:, 0:512], pD[:, 0:512], ALU.mult),
                    "D")
                tof = P.op("v", lambda e, kc=kc: e.tensor_tensor(
                    f3of[:, kc, 0:512], tmpn[:, 0:512], f3x2[:, kc, 0:512],
                    ALU.add), waits=[tok_x2], inc=sv)
            tof = fence("v")
            tok_x2_rd = tof
            for tc4 in range(4):
                bi = tc4 % 2
                r0 = t0 + ch * 512 + tc4 * 128
                cp_toks = []
                for half in range(2):
                    key = ["E", "F"][half]
                    pap = [pE, pF][half]
                    for j in range(4):
                        dt = half * 4 + j
                        trans(pap[:, j * 128:(j + 1) * 128],
                              f3of[:, dt, tc4 * 128:(tc4 + 1) * 128],
                              idf_s[:], key, waits=[tof])
                    cp_toks.append(pread(
                        "v", lambda e, bi=bi, half=half, pap=pap:
                        e.tensor_copy(f3tm[:, bi, half * 512:(half + 1) * 512],
                                      pap[:, 0:512]), key,
                        waits=[tok_tm_free[bi]]))
                ftm = fence("v")
                last = dma("y", outsg_h[r0:r0 + 128, :], f3tm[:, bi, :],
                           waits=[ftm])
                out_dma_toks.append(last)
                tok_tm_free[bi] = last
        tok_xn2s_rd = (sp, sp.v)
        tok_f3_act = (ss, ss.v)
        if tau == 1:
            dma("y", out_h[0:3 * (NR[0] - 1) + 1:3, :],
                outsg_h[0:NR[0], :], waits=[(sy, sy.v)])
        elif tau == 2:
            dma("y", out_h[1:3 * (NR[1] - 1) + 2:3, :],
                outsg_h[OFF[1]:OFF[1] + NR[1], :], waits=[(sy, sy.v)])

    dma("y", out_h[2:3 * (NR[2] - 1) + 3:3, :],
        outsg_h[OFF[2]:OFF[2] + NR[2], :], waits=[(sy, sy.v)])

    P.wait("y", (sy, sy.v))
    P.wait("g", (sg, sg.v))

    with nc.allow_low_precision(reason="bf16 matmul operands are sanctioned for this problem"), \
            nc.Block() as block:
        @block.tensor
        def _(t):
            for fn in P.q["t"]:
                fn(t)

        @block.vector
        def _(v):
            for fn in P.q["v"]:
                fn(v)

        @block.scalar
        def _(sc):
            for fn in P.q["s"]:
                fn(sc)

        @block.gpsimd
        def _(g):
            for fn in P.q["g"]:
                fn(g)

        @block.sync
        def _(s_):
            for fn in P.q["y"]:
                fn(s_)
    st.close()
    return nc


def _consts():
    idf = np.eye(128, dtype=np.float32)
    idb = idf.astype(ml_dtypes.bfloat16)
    strip = np.ones((128, 132), np.float32)
    for p in range(128):
        for c in range(p, p + 5):
            if 0 <= c < 132:
                strip[p, c] = 2.0
    fmask = np.ones((128, NB), np.float32)
    q_idx = np.arange(NB)
    for p in range(86):
        k = 1280 + p
        fmask[p] = np.where(np.abs(q_idx - k) <= 2, 2.0, 1.0)
    for p, act in ((86, [1364, 1365]), (87, [1365]), (88, [0, 1]), (89, [0])):
        row = np.zeros(NB, np.float32)
        row[act] = 1.0
        fmask[p] = row
    fmask[90:] = 0.0
    oh = np.zeros((8, 1024), np.float32)
    for e in range(8):
        oh[e, e * 128:(e + 1) * 128] = 1.0
    return (idf, idb, strip.astype(ml_dtypes.bfloat16),
            fmask.astype(ml_dtypes.bfloat16), oh.astype(ml_dtypes.bfloat16))


def kernel(L0_attn_x, L0_attn_W_q, L0_attn_W_k, L0_attn_W_v, L0_ffn_gate_w,
           L0_ffn_W1, L0_ffn_b1, L0_ffn_W2, L0_ffn_b2, L0_ffn_sW1, L0_ffn_sb1,
           L0_ffn_sW2, L0_ffn_sb2):
    f = np.float32
    x = np.ascontiguousarray(np.asarray(L0_attn_x, f))
    idf, idb, strip, fmask, oh = _consts()
    common = {
        "wq": np.ascontiguousarray(np.asarray(L0_attn_W_q, f)),
        "wk": np.ascontiguousarray(np.asarray(L0_attn_W_k, f)),
        "wv": np.ascontiguousarray(np.asarray(L0_attn_W_v, f)),
        "gwt": np.ascontiguousarray(np.asarray(L0_ffn_gate_w, f).T),
        "w1": np.ascontiguousarray(np.asarray(L0_ffn_W1, f)),
        "b1t": np.ascontiguousarray(np.concatenate(
            [np.asarray(L0_ffn_b1, f).T,
             np.asarray(L0_ffn_sb1, f)[:, None]], axis=1)),
        "w2": np.ascontiguousarray(np.asarray(L0_ffn_W2, f)),
        "b2": np.ascontiguousarray(np.asarray(L0_ffn_b2, f)),
        "sw1": np.ascontiguousarray(np.asarray(L0_ffn_sW1, f)),
        "sw2": np.ascontiguousarray(np.asarray(L0_ffn_sW2, f)),
        "sb2": np.ascontiguousarray(np.asarray(L0_ffn_sb2, f)),
        "idf": idf, "idb": idb, "strip": strip, "fmask": fmask, "oh": oh,
    }
    nc = build()
    in_maps = [dict(common, x=np.ascontiguousarray(x[b])) for b in range(8)]
    res = run_bass_kernel_spmd(nc, in_maps, core_ids=list(range(8)))
    return np.stack([np.asarray(res.results[b]["out"], f) for b in range(8)], 0)
